# version 14 (final)
# baseline (speedup 1.0000x reference)
import numpy as np
from contextlib import ExitStack

import concourse.bass as bass
import concourse.mybir as mybir
from concourse.alu_op_type import AluOpType as A
from concourse.bass_utils import run_bass_kernel_spmd

F32 = mybir.dt.float32
BF16 = mybir.dt.bfloat16
AF = mybir.ActivationFunctionType
AX = mybir.AxisListType.X
MAGIC = float(np.float32(1.5 * 2 ** 23))
NEG = float(np.float32(-3.0e38))
T = 2048
D = 1024
INV32 = float(np.float32(1.0 / 32.0))
ISQ512 = float(np.float32(512 ** -0.5))

_CACHE = {}


class Sem:
    def __init__(self, nc, es, name):
        self.h = es.enter_context(nc.semaphore(name))
        self.n = 0

    def inc(self, inst, k=1):
        inst.then_inc(self.h, k)
        self.n += k
        return inst

    def incd(self, inst):
        return self.inc(inst, 16)

    def wait(self, eng, val=None):
        eng.wait_ge(self.h, self.n if val is None else val)



def _w(e, sem):
    e.wait_ge(sem.h, sem.n)
    return False

class B:
    def __init__(self):
        nc = bass.Bass("TRN2", target_bir_lowering=False, debug=False)
        self.nc = nc
        self.es = ExitStack()
        d = {}
        self.d = d

        def din(name, shape, dt=F32):
            d[name] = nc.dram_tensor(name, shape, dt, kind="ExternalInput").ap()

        def dit(name, shape, dt=F32):
            d[name] = nc.dram_tensor(name, shape, dt, kind="Internal").ap()

        din("x_preT", [D, T]); din("x_postT", [D, T])
        din("Wq", [D, 4096]); din("Wk", [D, 4096]); din("Wv", [D, 4096])
        din("Wo", [4096, D])
        din("phase_T", [128, 32])
        din("bq_c", [128, 32]); din("bk_c", [128, 32]); din("bv_r", [1, 4096])
        din("bo_c", [128, 8]); din("bout_c", [128, 8])
        din("wup_sh", [384, D]); din("wgt_sh", [384, D]); din("wdn_sh", [128, 3072])
        din("wqkvT", [D, 1536]); din("woutT", [512, D])
        din("wg1T_sh", [128, 4096]); din("wu1T_sh", [128, 4096]); din("wd1T_sh", [512, D])
        din("identf", [128, 128])
        din("bdmask", [128, 128]); din("l1mask", [128, 4096])
        din("onescol", [128, 1]); din("onesrow", [1, 128])
        d["out_T"] = nc.dram_tensor("out_T", [D, T], F32, kind="ExternalOutput").ap()

        dit("vdump", [T, 4096])
        dit("updump", [3072, T]); dit("gatedump", [3072, T])
        dit("nupdump", [3072, T]); dit("pdump", [3072, T])
        dit("qdump", [3072, T]); dit("nodump", [3072, T])
        dit("q1dump", [512, T]); dit("nx1dump", [D, T], BF16)
        dit("rsc_dr", [16, 128]); dit("dqz_dr", [16, 128])
        dit("lnrow_dr", [2, 2048])
        for nm in ("arw", "ar1", "ar2", "ar3", "ar4"):
            dit(nm + "_i", [1, 4]); dit(nm + "_o", [1, 4])
        dit("agup_i", [D, 384]); dit("agup_o", [8, D, 384])
        dit("aggt_i", [D, 384]); dit("aggt_o", [8, D, 384])
        dit("agdn_i", [3072, 128]); dit("agdn_o", [8, 3072, 128])
        dit("agg1_i", [128, 4096], BF16); dit("agg1_o", [8, 128, 4096], BF16)
        dit("agu1_i", [128, 4096], BF16); dit("agu1_o", [8, 128, 4096], BF16)
        dit("agd1_i", [512, D], BF16); dit("agd1_o", [8, 512, D], BF16)
        dit("agk_i", [512, T]); dit("agk_o", [2, 512, T])
        dit("agv_i", [T, 512]); dit("agv_o", [2, T, 512])

        sb = lambda n, shp, dt=F32: self.es.enter_context(nc.sbuf_tensor(n, shp, dt))
        self.buf_a = sb("buf_a", [128, 8 * T])
        self.buf_b = sb("buf_b", [128, 8 * T])
        self.scr = sb("scr", [128, 13824])
        self.ident = sb("ident", [128, 128])
        self.bdm = sb("bdm", [128, 128])
        self.ones = sb("ones", [128, 1])
        self.onesr = sb("onesr", [1, 128])
        self.cosc = sb("cosc", [128, 32]); self.sinc = sb("sinc", [128, 32])
        self.phs = sb("phs", [128, 32])
        self.bqc = sb("bqc", [128, 32]); self.bkc = sb("bkc", [128, 32])
        self.boc = sb("boc", [128, 8]); self.bouc = sb("bouc", [128, 8])
        self.colp = sb("colp", [128, 16])
        self.scal = sb("scal", [1, 16])
        self.rm16 = sb("rm16", [128, 16])
        self.rm16b = sb("rm16b", [128, 16])
        self.rowbuf = sb("rowbuf", [128, 4096])
        self.mrow = self.rowbuf[0:1, 0:2048]; self.vrow = self.rowbuf[0:1, 2048:4096]
        self.nmb = sb("nmb", [128, 512]); self.rsb = sb("rsb", [128, 512])
        self.sqch = sb("sqch", [128, 512])
        self.pb = [self.es.enter_context(nc.psum_tensor(f"pb{i}", [128, 512], F32))
                   for i in range(8)]
        self._sems = {}


    MERGE_GROUPS = {
        "gw": ["wld", "wld2", "wld3", "wldd", "wsumc", "wsmm", "wsrow",
               "arwo", "l1s", "l1smm", "l1mcols",
               "trig", "hpi", "c0", "xld", "xpld", "xnq", "stout", "scz",
               "lnvar", "lnpvar", "lnsqr", "lnrows",
               "ln0done", "lnpdone", "ln1done"],
        "gax": ["axtp", "axred", "axoo", "axbc", "ar1col", "ar2col",
                "ar3col", "ar4col", "ar2cols", "ar3cols", "ar4cols", "rxcol",
                "nx1tp", "nx1rs"],
        "gfv": ["f2v", "f3v", "f4v"],
        "gfld": ["f2ld", "f3ld", "f4ld"],
        "gfdmp": ["f2dmp", "f3dmp", "f4dmp"],
        "gsw": ["azr", "arwi", "axii", "lnrdr"],
        "nxbc": ["nx1bci", "nx1bc"],
    }
    MERGE = {}
    for _g, _names in MERGE_GROUPS.items():
        for _n in _names:
            MERGE[_n] = _g

    def S(self, name):
        phys = self.MERGE.get(name, name)
        if phys not in self._sems:
            self._sems[phys] = Sem(self.nc, self.es, phys)
        return self._sems[phys]

    def sc(self, off, size):
        assert off + size <= 13824, (off, size)
        return self.scr[:, off:off + size]

    CL = {"rx": 0, "sx": 1, "cup": 2, "cgt": 3, "sgt": 4, "c2r2": 5, "c3ro": 6,
          "so": 7, "mg1": 8, "mu1": 9, "md1": 10, "t0": 11, "t1": 12, "sup": 13,
          "c2": 14, "c3": 15}

    def col(self, k):
        c = self.CL[k]
        return self.colp[:, c:c + 1]

    def ln(self, block, buf, dst, tag, gate, pregate=(), maxcol=None):
        S, pb = self.S, self.pb
        ones, mrow, vrow, nmb, rsb, sqch = (self.ones, self.mrow, self.vrow,
                                            self.nmb, self.rsb, self.sqch)
        bsq = S("lnsq").n
        bs2m = S("lns2m").n
        bnrm = S("lnnrm").n
        bbc = S("lnbc").n

        @block.scalar
        def _(e):
            gate.wait(e)
            for kc in range(8):
                for hf in range(4):
                    idx = kc * 4 + hf
                    if idx > 0:
                        e.wait_ge(S("lns2m").h, bs2m + idx)
                    i = e.activation(out=sqch[:, :],
                                     in_=buf[:, kc * T + hf * 512:
                                             kc * T + (hf + 1) * 512],
                                     func=AF.Square)
                    S("lnsq").inc(i)
                    e.drain()

        @block.tensor
        def _(e):
            gate.wait(e)
            for p in pregate:
                S(p).wait(e)
            for kc in range(8):
                for ns in range(4):
                    e.matmul(pb[ns][0:1, :], ones[:, :],
                             buf[:, kc * T + ns * 512: kc * T + (ns + 1) * 512],
                             start=(kc == 0), stop=(kc == 7))
                for hf in range(4):
                    e.wait_ge(S("lnsq").h, bsq + kc * 4 + hf + 1)
                    i = e.matmul(pb[4 + hf][0:1, :], ones[:, :],
                                 sqch[:, :],
                                 start=(kc == 0), stop=(kc == 7))
                    S("lns2m").inc(i)

        @block.vector
        def _(e):
            S("lns2m").wait(e)
            for ns in range(4):
                e.tensor_scalar(out=mrow[0:1, ns * 512:(ns + 1) * 512],
                                in0=pb[ns][0:1, :], scalar1=float(1.0 / D),
                                scalar2=0.0, op0=A.mult, op1=A.add)
                e.drain()
                e.tensor_scalar(out=vrow[0:1, ns * 512:(ns + 1) * 512],
                                in0=pb[4 + ns][0:1, :], scalar1=float(1.0 / D),
                                scalar2=0.0, op0=A.mult, op1=A.add)
                e.drain()
            for ns in range(4):
                seg = slice(ns * 512, (ns + 1) * 512)
                e.scalar_tensor_tensor(out=sqch[0:1, 0:512], in0=mrow[0:1, seg],
                                       scalar=-1.0, in1=mrow[0:1, seg],
                                       op0=A.mult, op1=A.mult)
                e.drain()
                e.tensor_tensor(out=vrow[0:1, seg], in0=vrow[0:1, seg],
                                in1=sqch[0:1, 0:512], op=A.add)
                e.drain()
            e.tensor_scalar(out=vrow[0:1, :], in0=vrow[0:1, :], scalar1=1e-5,
                            scalar2=0.0, op0=A.add, op1=A.add)
            e.drain()
            S("lnvar" if tag != "lnp" else "lnpvar").inc(e.nop())

        @block.scalar
        def _(e):
            S("lnvar" if tag != "lnp" else "lnpvar").wait(e)
            i = e.activation(out=vrow[0:1, :], in_=vrow[0:1, :], func=AF.Sqrt)
            S("lnsqr").inc(i)
            e.drain()

        @block.vector
        def _(e):
            S("lnsqr").wait(e)
            e.reciprocal(vrow[0:1, :], vrow[0:1, :])
            e.drain()
            e.tensor_scalar(out=mrow[0:1, :], in0=mrow[0:1, :], scalar1=-1.0,
                            scalar2=0.0, op0=A.mult, op1=A.add)
            e.drain()
            S("lnrows").inc(e.nop())

        @block.gpsimd
        def _(e):
            S("lnrows").wait(e)
            S("lnrdr").incd(e.dma_start(out=self.d["lnrow_dr"][0:1, :],
                                        in_=self.rowbuf[0:1, 0:2048]))
            S("lnrdr").incd(e.dma_start(out=self.d["lnrow_dr"][1:2, :],
                                        in_=self.rowbuf[0:1, 2048:4096]))
            S("lnrdr").wait(e)
            for ns in range(4):
                if ns > 0:
                    e.wait_ge(S("lnnrm").h, bnrm + ns * 16)
                s2 = S("lnbc")
                _w(e, s2) or s2.incd(e.dma_start(out=nmb[:, :],
                                    in_=self.d["lnrow_dr"][0:1, ns * 512:(ns + 1) * 512]
                                    .partition_broadcast(128)))
                _w(e, s2) or s2.incd(e.dma_start(out=rsb[:, :],
                                    in_=self.d["lnrow_dr"][1:2, ns * 512:(ns + 1) * 512]
                                    .partition_broadcast(128)))

        @block.vector
        def _(e):
            for ns in range(4):
                e.wait_ge(S("lnbc").h, bbc + 32 * (ns + 1))
                for kc in range(8):
                    si = buf[:, kc * T + ns * 512: kc * T + (ns + 1) * 512]
                    so = dst[:, kc * T + ns * 512: kc * T + (ns + 1) * 512]
                    e.tensor_tensor(out=so, in0=si, in1=nmb[:, :], op=A.add)
                    e.drain()
                    e.tensor_tensor(out=so, in0=so, in1=rsb[:, :], op=A.mult)
                    e.drain()
                    if maxcol is not None:
                        e.tensor_reduce(out=self.rm16b[:, 15:16], in_=so, axis=AX,
                                        op=A.max, apply_absolute_value=True)
                        e.drain()
                        if ns == 0 and kc == 0:
                            e.tensor_copy(maxcol, self.rm16b[:, 15:16])
                        else:
                            e.tensor_tensor(out=maxcol, in0=maxcol,
                                            in1=self.rm16b[:, 15:16], op=A.max)
                        e.drain()
                    S("lnnrm").inc(e.nop(), 2)
            S(tag + "done").inc(e.nop())

    def arx(self, block, tag, incol, ncols, dstcols, gate):
        S, pb, d, s = self.S, self.pb, self.d, self

        @block.tensor
        def _(e):
            gate.wait(e)
            i = e.transpose(pb[6][0:ncols, 0:128], incol[:, 0:ncols], s.ident[:, :])
            S("axtp").inc(i)

        @block.vector
        def _(e):
            S("axtp").wait(e)
            e.tensor_reduce(out=s.rm16b[0:ncols, 14:15], in_=pb[6][0:ncols, 0:128],
                            axis=AX, op=A.max)
            e.drain()
            S("axred").inc(e.nop())

        @block.gpsimd
        def _(e):
            S("axred").wait(e)
            S("axii").incd(e.dma_start(
                out=d[tag + "_i"][0:1, 0:ncols].rearrange("a b -> b a"),
                in_=s.rm16b[0:ncols, 14:15]))
            S("axii").wait(e)
            S("cc").inc(e.collective_compute(
                "AllReduce", A.max, replica_groups=[list(range(8))],
                ins=[d[tag + "_i"].opt()], outs=[d[tag + "_o"].opt()]))

        @block.scalar
        def _(e):
            self.S("cc").wait(e)
            S("axoo").incd(e.dma_start(out=s.scal[0:1, 8:8 + ncols],
                                       in_=d[tag + "_o"][0:1, 0:ncols]))

        @block.tensor
        def _(e):
            S("axoo").wait(e)
            i = e.matmul(pb[6][:, 0:ncols], s.onesr[:, :], s.scal[0:1, 8:8 + ncols],
                         start=True, stop=True)
            S("axbc").inc(i)

        @block.vector
        def _(e):
            S("axbc").wait(e)
            for j, dc in enumerate(dstcols):
                e.tensor_copy(dc, pb[6][:, j:j + 1])
                e.drain()
            S(tag + "col").inc(e.nop())

    def ph0(self, block):
        S, s, d = self.S, self, self.d

        @block.sync
        def _(e):
            c0 = S("c0")
            for dst, src in ((s.ident, d["identf"]), (s.bdm, d["bdmask"]),
                             (s.ones, d["onescol"]), (s.onesr, d["onesrow"]),
                             (s.phs, d["phase_T"]), (s.bqc, d["bq_c"]),
                             (s.bkc, d["bk_c"]), (s.boc, d["bo_c"]),
                             (s.bouc, d["bout_c"])):
                c0.incd(e.dma_start(out=dst[:, :], in_=src[:, :]))
            S("xld").incd(e.dma_start(
                out=s.buf_a[:, :].rearrange("p (c t) -> p c t", c=8),
                in_=d["x_preT"][:, :].rearrange("(c p) t -> p c t", p=128)))
            wl = S("wld")
            wl.incd(e.dma_start(out=s.sc(0, 4096), in_=d["wg1T_sh"][:, :]))
            wl.incd(e.dma_start(out=s.sc(4096, 4096), in_=d["wu1T_sh"][:, :]))
            w2 = S("wld2")
            w2.incd(e.dma_start(out=s.sc(8192, 3072).rearrange("p (a b) -> p a b", a=3),
                                in_=d["wup_sh"][:, :].rearrange("(a p) b -> p a b", p=128)))
            w2.incd(e.dma_start(out=s.buf_b[:, 4608:7680].rearrange("p (a b) -> p a b", a=3),
                                in_=d["wgt_sh"][:, :].rearrange("(a p) b -> p a b", p=128)))
            S("wldd").incd(e.dma_start(
                out=s.buf_b[:, 0:4096].rearrange("p (a b) -> p a b", a=4),
                in_=d["wd1T_sh"][:, :].rearrange("(a p) b -> p a b", p=128)))

        @block.vector
        def _(e):
            i = e.memset(s.colp[:, 12:13], float(np.float32(np.pi / 2)))
            S("hpi").inc(i)
            e.drain()
            i = e.memset(s.scal[0:1, 0:16], 0.0)
            S("scz").inc(i)
            e.drain()

        @block.gpsimd
        def _(e):
            S("scz").wait(e)
            for nm in ("arw_i", "ar1_i", "ar2_i", "ar3_i", "ar4_i"):
                S("azr").incd(e.dma_start(out=d[nm][0:1, 0:4], in_=s.scal[0:1, 4:8]))
            S("azr").wait(e)

        @block.scalar
        def _(e):
            S("c0").wait(e)
            e.activation(out=s.sinc[:, :], in_=s.phs[:, :], func=AF.Sin)
            e.drain()
            S("hpi").wait(e)
            i = e.activation(out=s.cosc[:, :], in_=s.phs[:, :], func=AF.Sin,
                             bias=s.colp[:, 12:13])
            S("trig").inc(i)
            e.drain()

    def phW(self, block):
        S, s, d, pb = self.S, self, self.d, self.pb

        @block.vector
        def _(e):
            S("wld").wait(e)
            e.tensor_reduce(out=s.rm16[:, 0:1], in_=s.sc(0, 4096), axis=AX,
                            op=A.add, apply_absolute_value=True)
            e.drain()
            e.tensor_reduce(out=s.rm16[:, 1:2], in_=s.sc(4096, 4096), axis=AX,
                            op=A.add, apply_absolute_value=True)
            e.drain()
            S("wldd").wait(e)
            e.tensor_reduce(out=s.rm16[:, 2:3], in_=s.buf_b[:, 0:4096], axis=AX,
                            op=A.add, apply_absolute_value=True)
            e.drain()
            S("wsumc").inc(e.nop())

        @block.tensor
        def _(e):
            S("wsumc").wait(e)
            S("wsmm").inc(e.matmul(pb[0][0:1, 0:3], s.ones[:, :], s.rm16[:, 0:3],
                                   start=True, stop=True))

        @block.vector
        def _(e):
            S("wsmm").wait(e)
            e.tensor_copy(s.scal[0:1, 0:3], pb[0][0:1, 0:3])
            e.drain()
            S("wsrow").inc(e.nop())

        @block.gpsimd
        def _(e):
            S("wsrow").wait(e)
            S("arwi").incd(e.dma_start(out=d["arw_i"][0:1, 0:3], in_=s.scal[0:1, 0:3]))
            S("arwi").wait(e)
            S("cc").inc(e.collective_compute(
                "AllReduce", A.add, replica_groups=[list(range(8))],
                ins=[d["arw_i"].opt()], outs=[d["arw_o"].opt()]))

        @block.scalar
        def _(e):
            S("cc").wait(e, 1)
            S("arwo").incd(e.dma_start(out=s.scal[0:1, 4:7], in_=d["arw_o"][0:1, 0:3]))

        @block.vector
        def _(e):
            S("arwo").wait(e)
            e.tensor_scalar(out=s.scal[0:1, 4:7], in0=s.scal[0:1, 4:7],
                            scalar1=float(1.0 / (4096 * 1024)), scalar2=1e-5,
                            op0=A.mult, op1=A.max)
            e.drain()
            e.reciprocal(s.scal[0:1, 8:11], s.scal[0:1, 4:7])
            e.drain()
            S("l1s").inc(e.nop())

        @block.tensor
        def _(e):
            S("l1s").wait(e)
            S("l1smm").inc(e.matmul(pb[1][:, 0:12], s.onesr[:, :], s.scal[0:1, 0:12],
                                    start=True, stop=True))

        @block.vector
        def _(e):
            S("l1smm").wait(e)
            e.tensor_copy(s.rm16[:, 0:12], pb[1][:, 0:12])
            e.drain()
            e.tensor_copy(s.col("mg1"), s.rm16[:, 4:5])
            e.drain()
            e.tensor_copy(s.col("mu1"), s.rm16[:, 5:6])
            e.drain()
            e.tensor_copy(s.col("md1"), s.rm16[:, 6:7])
            e.drain()
            S("l1mcols").inc(e.nop())
            for widx, (src_of, nst, sci) in enumerate(((0, 8, 8), (4096, 8, 9))):
                for st in range(nst):
                    if widx * 8 + st >= 2:
                        S("tdmp").wait(e, 16 * (widx * 8 + st - 1))
                    src = s.sc(src_of + st * 512, 512)
                    slot = 4096 + 256 * ((widx * 8 + st) % 2)
                    dstb = s.buf_b[:, slot:slot + 256].bitcast(BF16)
                    e.tensor_scalar(out=src, in0=src, scalar1=s.rm16[:, sci:sci + 1],
                                    scalar2=MAGIC, op0=A.mult, op1=A.add)
                    e.drain()
                    e.tensor_scalar(out=src, in0=src, scalar1=MAGIC, scalar2=1.0,
                                    op0=A.subtract, op1=A.min)
                    e.drain()
                    e.tensor_scalar(out=dstb, in0=src, scalar1=-1.0, scalar2=0.0,
                                    op0=A.max, op1=A.add)
                    e.drain()
                    S("tst").inc(e.nop())
            for st in range(8):
                gi = 16 + st
                S("tdmp").wait(e, 16 * (gi - 1))
                src = s.buf_b[:, st * 512:(st + 1) * 512]
                slot = 4096 + 256 * (gi % 2)
                dstb = s.buf_b[:, slot:slot + 256].bitcast(BF16)
                e.tensor_scalar(out=src, in0=src, scalar1=s.rm16[:, 10:11],
                                scalar2=MAGIC, op0=A.mult, op1=A.add)
                e.drain()
                e.tensor_scalar(out=src, in0=src, scalar1=MAGIC, scalar2=1.0,
                                op0=A.subtract, op1=A.min)
                e.drain()
                e.tensor_scalar(out=dstb, in0=src, scalar1=-1.0, scalar2=0.0,
                                op0=A.max, op1=A.add)
                e.drain()
                S("tst").inc(e.nop())

        @block.gpsimd
        def _(e):
            for gi in range(24):
                S("tst").wait(e, gi + 1)
                slot = 4096 + 256 * (gi % 2)
                srcb = s.buf_b[:, slot:slot + 256].bitcast(BF16)
                widx, st = gi // 8, gi % 8
                if widx == 0:
                    dst = d["agg1_i"][:, st * 512:(st + 1) * 512]
                elif widx == 1:
                    dst = d["agu1_i"][:, st * 512:(st + 1) * 512]
                else:
                    dst = d["agd1_i"][:, :].rearrange("(a p) b -> p a b", p=128)[
                        :, st // 2, (st % 2) * 512:(st % 2 + 1) * 512]
                _w(e, S("tdmp")) or S("tdmp").incd(e.dma_start(out=dst, in_=srcb))
            S("tdmp").wait(e)
            for agi, ago in (("agg1", "agg1"), ("agu1", "agu1"), ("agd1", "agd1")):
                S("cc").inc(e.collective_compute(
                    "AllGather", A.bypass, replica_groups=[list(range(8))],
                    ins=[d[agi + "_i"].opt()], outs=[d[ago + "_o"].opt()]))

        @block.sync
        def _(e):
            S("tst").wait(e)
            S("wld3").incd(e.dma_start(out=s.buf_b[:, 0:3072], in_=d["wdn_sh"][:, :]))

        @block.vector
        def _(e):
            S("wld2").wait(e)
            S("wld3").wait(e)
            wsrcs = (lambda it: s.sc(8192 + it * D, D),
                     lambda it: s.buf_b[:, 4608 + it * D:4608 + (it + 1) * D],
                     lambda it: s.buf_b[:, 0:3072])
            for widx2, (ntile, I) in enumerate(((3, D), (3, D), (1, 3072))):
                nb = I // 32
                for it in range(ntile):
                    w = wsrcs[widx2](it)
                    mx = s.sc(6144, nb)
                    rcp = s.sc(6144 + nb, nb)
                    e.tensor_reduce(out=mx, in_=w.rearrange("p (b r) -> p b r", r=32),
                                    axis=AX, op=A.max, apply_absolute_value=True)
                    e.drain()
                    e.tensor_scalar(out=mx, in0=mx, scalar1=0.1428571492433548, scalar2=1e-12,
                                    op0=A.mult, op1=A.max)
                    e.drain()
                    e.reciprocal(rcp, mx)
                    e.drain()
                    wv = w.rearrange("p (b r) -> p b r", r=32)
                    e.tensor_tensor(out=wv, in0=wv,
                                    in1=rcp.unsqueeze(2).broadcast_to([128, nb, 32]),
                                    op=A.mult)
                    e.drain()
                    e.tensor_scalar(out=w, in0=w, scalar1=MAGIC, scalar2=MAGIC,
                                    op0=A.add, op1=A.subtract)
                    e.drain()
                    e.tensor_tensor(out=wv, in0=wv,
                                    in1=mx.unsqueeze(2).broadcast_to([128, nb, 32]),
                                    op=A.mult)
                    e.drain()
                    S("deq").inc(e.nop())

        @block.tensor
        def _(e):
            cnt = 0
            wsrcs = (lambda it: s.sc(8192 + it * D, D),
                     lambda it: s.buf_b[:, 4608 + it * D:4608 + (it + 1) * D],
                     lambda it: s.buf_b[:, 0:3072])
            for widx, (ntile, I) in enumerate(((3, D), (3, D), (1, 3072))):
                for it in range(ntile):
                    S("deq").wait(e, widx * 3 + it + 1 if widx < 2 else 7)
                    for ic in range(I // 128):
                        if cnt >= 4:
                            S("wtpe").wait(e, cnt - 3)
                        w = wsrcs[widx](it)
                        i = e.matmul(pb[cnt % 4][:, 0:128],
                                     w[:, ic * 128:(ic + 1) * 128], s.ident[:, :],
                                     start=True, stop=True, is_transpose=True)
                        S("wtp").inc(i)
                        cnt += 1

        @block.vector
        def _(e):
            cnt = 0
            for widx, (ntile, I) in enumerate(((3, D), (3, D), (1, 3072))):
                nic = I // 128
                for it in range(ntile):
                    for ic in range(nic):
                        cnt += 1
                        if widx >= 1 and it == 0 and ic == 0:
                            e.wait_ge(S("wdmp").h, 16 * widx)
                        S("wtp").wait(e, cnt)
                        if widx < 2:
                            dst = s.sc(3072 + (ic * 3 + it) * 128, 128)
                        else:
                            dst = s.sc(3072 + ic * 128, 128)
                        e.tensor_copy(dst, pb[(cnt - 1) % 4][:, 0:128])
                        e.drain()
                        S("wtpe").inc(e.nop())
                if widx < 2:
                    S("wstg").inc(e.nop())
            S("wstg").inc(e.nop())

        @block.gpsimd
        def _(e):
            e.wait_ge(S("wstg").h, 1)
            _w(e, S("wdmp")) or S("wdmp").incd(e.dma_start(
                out=d["agup_i"][:, :].rearrange("(c p) o -> p c o", p=128),
                in_=s.sc(3072, 3072).rearrange("p (c o) -> p c o", c=8)))
            e.wait_ge(S("wstg").h, 2)
            _w(e, S("wdmp")) or S("wdmp").incd(e.dma_start(
                out=d["aggt_i"][:, :].rearrange("(c p) o -> p c o", p=128),
                in_=s.sc(3072, 3072).rearrange("p (c o) -> p c o", c=8)))
            e.wait_ge(S("wstg").h, 3)
            _w(e, S("wdmp")) or S("wdmp").incd(e.dma_start(
                out=d["agdn_i"][:, :].rearrange("(c p) o -> p c o", p=128),
                in_=s.sc(3072, 3072).rearrange("p (c o) -> p c o", c=24)))
            S("wdmp").wait(e)
            for agn in ("agup", "aggt", "agdn"):
                S("cc").inc(e.collective_compute(
                    "AllGather", A.bypass, replica_groups=[list(range(8))],
                    ins=[d[agn + "_i"].opt()], outs=[d[agn + "_o"].opt()]))

    def phLN0(self, block):
        S, s = self.S, self
        self.ln(block, s.buf_a, s.buf_a, "ln0", S("xld"),
                pregate=("wtpe", "l1mcols"), maxcol=s.rm16b[:, 0:1])

    def phQuarter(self, block, q):
        S, s, d, pb = self.S, self, self.d, self.pb
        QR, QI, KR, KI = (s.sc(0, 2048), s.sc(2048, 2048),
                          s.sc(4096, 2048), s.sc(6144, 2048))
        SSB = s.sc(10240, 2048)
        PSC = s.sc(12288, 128)

        @block.sync
        def _(e):
            if q > 0:
                e.wait_ge(S("woev").h, 16 * q)
            else:
                e.wait_ge(S("wtp").h, 56)
            for dc in range(8):
                for side in range(8):
                    gi = (q * 8 + dc) * 8 + side
                    W = d["Wq"] if side < 4 else d["Wk"]
                    h = side % 4
                    if gi >= 2:
                        e.wait_ge(S("qkmm").h, gi - 1)
                    _w(e, S("wslld")) or S("wslld").incd(e.dma_start(
                        out=s.sc(8192 + (gi % 2) * 1024, 1024)
                        .rearrange("p (c m) -> p c m", c=8),
                        in_=W[:, (h * 8 + dc) * 128:(h * 8 + dc + 1) * 128]
                        .rearrange("(c p) m -> p c m", p=128)))

        @block.tensor
        def _(e):
            if q == 0:
                S("ln0done").wait(e)
            else:
                e.wait_ge(S("woev").h, 16 * q)
                e.wait_ge(S("vevst").h, 32 * q)
            for dc in range(8):
                for side in range(8):
                    gi = (q * 8 + dc) * 8 + side
                    e.wait_ge(S("wslld").h, 16 * (gi + 1))
                    if gi >= 2:
                        e.wait_ge(S("qkev").h, 2 * (gi - 1))
                    for kc in range(8):
                        i = e.matmul(pb[gi % 2][:, :],
                                     s.sc(8192 + (gi % 2) * 1024 + kc * 128, 128),
                                     s.buf_a[:, kc * T + q * 512: kc * T + (q + 1) * 512],
                                     start=(kc == 0), stop=(kc == 7))
                    S("qkmm").inc(i)
                e.wait_ge(S("qkev").h, ((q * 8 + dc) + 1) * 16)
                for G in range(16):
                    sg = (q * 8 + dc) * 16 + G
                    if sg >= 2:
                        e.wait_ge(S("scev").h, sg - 1)
                    e.matmul(pb[2 + sg % 2][:, 0:128], QR[:, G * 128:(G + 1) * 128],
                             KR[:, G * 128:(G + 1) * 128], start=True, stop=False)
                    i = e.matmul(pb[2 + sg % 2][:, 0:128], QI[:, G * 128:(G + 1) * 128],
                                 KI[:, G * 128:(G + 1) * 128], start=False, stop=True)
                    S("scmm").inc(i)

        @block.vector
        def _(e):
            if q == 0:
                S("trig").wait(e)
                S("c0").wait(e)
            else:
                S("avmm").wait(e)
            for dc in range(8):
                for side in range(8):
                    gi = (q * 8 + dc) * 8 + side
                    S("qkmm").wait(e, gi + 1)
                    h = side % 4
                    cb = (s.bqc if side < 4 else s.bkc)[:, h * 8 + dc:h * 8 + dc + 1]
                    dr = (QR if side < 4 else KR).rearrange(
                        "p (t hh) -> p t hh", hh=4)[:, :, h]
                    di = (QI if side < 4 else KI).rearrange(
                        "p (t hh) -> p t hh", hh=4)[:, :, h]
                    e.tensor_scalar(out=dr, in0=pb[gi % 2][:, :], scalar1=cb,
                                    scalar2=s.cosc[:, h * 8 + dc:h * 8 + dc + 1],
                                    op0=A.add, op1=A.mult)
                    e.drain()
                    e.tensor_scalar(out=di, in0=pb[gi % 2][:, :], scalar1=cb,
                                    scalar2=s.sinc[:, h * 8 + dc:h * 8 + dc + 1],
                                    op0=A.add, op1=A.mult)
                    e.drain()
                    S("qkev").inc(e.nop(), 2)
                for G in range(16):
                    sg = (q * 8 + dc) * 16 + G
                    S("scmm").wait(e, sg + 1)
                    if dc == 0:
                        e.scalar_tensor_tensor(out=SSB[:, G * 128:(G + 1) * 128],
                                               in0=pb[2 + sg % 2][:, 0:128],
                                               scalar=INV32, in1=s.bdm[:, :],
                                               op0=A.mult, op1=A.add)
                    else:
                        e.scalar_tensor_tensor(out=SSB[:, G * 128:(G + 1) * 128],
                                               in0=pb[2 + sg % 2][:, 0:128],
                                               scalar=INV32,
                                               in1=SSB[:, G * 128:(G + 1) * 128],
                                               op0=A.mult, op1=A.add)
                    e.drain()
                    S("scev").inc(e.nop())

        @block.vector
        def _(e):
            for G in range(16):
                g16 = q * 16 + G
                if g16 >= 1:
                    S("sme").wait(e, g16)
                e.tensor_reduce(out=s.rm16b[:, 14:15], in_=SSB[:, G * 128:(G + 1) * 128],
                                axis=AX, op=A.max, negate=True)
                e.drain()
                S("smx").inc(e.nop())
                S("sme").wait(e, g16 + 1)
                e.reciprocal(s.rm16b[:, 12:13], s.rm16b[:, 13:14])
                e.drain()
                e.tensor_scalar(out=SSB[:, G * 128:(G + 1) * 128], in0=PSC,
                                scalar1=s.rm16b[:, 12:13], scalar2=0.0,
                                op0=A.mult, op1=A.add)
                e.drain()
                S("smn").inc(e.nop())
                S("smt").wait(e, g16 + 1)
                e.tensor_copy(SSB[:, G * 128:(G + 1) * 128],
                              pb[6 + g16 % 2][:, 0:128])
                e.drain()
                S("smev").inc(e.nop())

        @block.scalar
        def _(e):
            for G in range(16):
                g16 = q * 16 + G
                S("smx").wait(e, g16 + 1)
                i = e.activation(out=PSC, in_=SSB[:, G * 128:(G + 1) * 128],
                                 func=AF.Exp, bias=s.rm16b[:, 14:15],
                                 accum_out=s.rm16b[:, 13:14])
                S("sme").inc(i)
                e.drain()

        @block.tensor
        def _(e):
            for G in range(16):
                g16 = q * 16 + G
                S("smn").wait(e, g16 + 1)
                if g16 >= 2:
                    S("smev").wait(e, g16 - 1)
                i = e.transpose(pb[6 + g16 % 2][:, 0:128],
                                SSB[:, G * 128:(G + 1) * 128], s.ident[:, :])
                S("smt").inc(i)

        @block.sync
        def _(e):
            for nf in range(8):
                vi = q * 8 + nf
                if q == 0 and nf == 0:
                    e.wait_ge(S("tst").h, 24)
                e.wait_ge(S("scev").h, (q * 8 + 8) * 16)
                if vi >= 2:
                    e.wait_ge(S("vmm").h, 4 * (vi - 1))
                _w(e, S("wvld")) or S("wvld").incd(e.dma_start(
                    out=s.sc((vi % 2) * 4096, 4096)
                    .rearrange("p (c m) -> p c m", c=8),
                    in_=d["Wv"][:, nf * 512:(nf + 1) * 512]
                    .rearrange("(c p) m -> p c m", p=128)))

        @block.scalar
        def _(e):
            for nf in range(8):
                vi = q * 8 + nf
                if vi >= 1:
                    e.wait_ge(S("vevst").h, 4 * vi)
                _w(e, S("bvld")) or S("bvld").incd(e.dma_start(
                    out=s.sc(8192, 512),
                    in_=d["bv_r"][0:1, nf * 512:(nf + 1) * 512]
                    .partition_broadcast(128)))

        @block.tensor
        def _(e):
            if q == 0:
                pass
            e.wait_ge(S("qkev").h, (q + 1) * 128)
            for nf in range(8):
                vi = q * 8 + nf
                e.wait_ge(S("wvld").h, 16 * (vi + 1))
                for mt in range(4):
                    vidx = vi * 4 + mt
                    if vidx >= 2:
                        e.wait_ge(S("vevst").h, vidx - 1)
                    for kc in range(8):
                        i = e.matmul(pb[vidx % 2][:, :],
                                     s.buf_a[:, kc * T + q * 512 + mt * 128:
                                             kc * T + q * 512 + (mt + 1) * 128],
                                     s.sc((vi % 2) * 4096 + kc * 512, 512),
                                     start=(kc == 0), stop=(kc == 7))
                    S("vmm").inc(i)

        @block.vector
        def _(e):
            for nf in range(8):
                vi = q * 8 + nf
                S("bvld").wait(e, 16 * (vi + 1))
                for mt in range(4):
                    vidx = vi * 4 + mt
                    S("vmm").wait(e, vidx + 1)
                    if vidx >= 1:
                        S("vdmp").wait(e, 16 * vidx)
                    e.scalar_tensor_tensor(out=s.sc(8704, 512), in0=pb[vidx % 2][:, :],
                                           scalar=0.0, in1=s.sc(8192, 512),
                                           op0=A.add, op1=A.add)
                    e.drain()
                    S("vevst").inc(e.nop())

        @block.gpsimd
        def _(e):
            for nf in range(8):
                for mt in range(4):
                    vidx = (q * 8 + nf) * 4 + mt
                    S("vevst").wait(e, vidx + 1)
                    _w(e, S("vdmp")) or S("vdmp").incd(e.dma_start(
                        out=d["vdump"][q * 512 + mt * 128: q * 512 + (mt + 1) * 128,
                                       nf * 512:(nf + 1) * 512],
                        in_=s.sc(8704, 512)))

        @block.gpsimd
        def _(e):
            for half8 in range(2):
                B8 = q * 2 + half8
                for g8 in range(8):
                    Gg = B8 * 8 + g8
                    e.wait_ge(S("vdmp").h, 16 * 32 * (q + 1))
                    if Gg >= 1:
                        e.wait_ge(S("avmm").h, 8 * Gg)
                    _w(e, S("vreld")) or S("vreld").incd(e.dma_start(
                        out=s.sc(12416, 1024),
                        in_=d["vdump"][32 * Gg:32 * Gg + 32, :]
                        .rearrange("t (g dd) -> (t g) dd", g=4)))

        @block.tensor
        def _(e):
            e.wait_ge(S("smev").h, (q + 1) * 16)
            for half8 in range(2):
                B8 = q * 2 + half8
                for g8 in range(8):
                    Gg = B8 * 8 + g8
                    e.wait_ge(S("vreld").h, 16 * (Gg + 1))
                    if Gg >= 1:
                        e.wait_ge(S("avev").h, 2 * Gg)
                    for dc in range(8):
                        i = e.matmul(
                            pb[4 + dc // 4][:, (dc % 4) * 128:(dc % 4 + 1) * 128],
                            s.sc(12416 + dc * 128, 128),
                            SSB[:, (half8 * 8 + g8) * 128:(half8 * 8 + g8 + 1) * 128],
                            start=True, stop=True)
                        S("avmm").inc(i)
                e.wait_ge(S("avev").h, 16 * (B8 + 1))
                for oc in range(8):
                    wi = B8 * 8 + oc
                    if wi >= 2:
                        e.wait_ge(S("woev").h, wi - 1)
                    e.wait_ge(S("wold").h, 32 * wi + 16)
                    for kk in range(16):
                        jv, dcv = kk // 8, kk % 8
                        i = e.matmul(pb[2 + oc % 2][:, 0:256],
                                     s.sc(8192 + kk * 128, 128),
                                     s.sc((dcv * 4 + jv) * 256, 256),
                                     start=(kk == 0), stop=False)
                    S("womm").inc(i)
                    e.wait_ge(S("wold").h, 32 * wi + 32)
                    for kk in range(16, 32):
                        jv, dcv = kk // 8, kk % 8
                        i = e.matmul(pb[2 + oc % 2][:, 0:256],
                                     s.sc(8192 + (kk - 16) * 128, 128),
                                     s.sc((dcv * 4 + jv) * 256, 256),
                                     start=False, stop=(kk == 31))
                    S("womm").inc(i)

        @block.vector
        def _(e):
            for half8 in range(2):
                B8 = q * 2 + half8
                for g8 in range(8):
                    Gg = B8 * 8 + g8
                    S("avmm").wait(e, 8 * (Gg + 1))
                    ob = s.sc(0, 8192).rearrange(
                        "p (dc j g8 a h) -> p dc g8 a j h", dc=8, j=4, g8=8, a=8)
                    e.tensor_copy(ob[:, 0:4, g8],
                                  pb[4][:, :].rearrange("p (dcl a j h) -> p dcl a j h",
                                                        dcl=4, a=8, j=4))
                    e.drain()
                    e.tensor_copy(ob[:, 4:8, g8],
                                  pb[5][:, :].rearrange("p (dcl a j h) -> p dcl a j h",
                                                        dcl=4, a=8, j=4))
                    e.drain()
                    S("avev").inc(e.nop(), 2)
                for oc in range(8):
                    wi = B8 * 8 + oc
                    S("womm").wait(e, 2 * wi + 2)
                    dst = s.buf_b.rearrange(
                        "p (dmc h bb g8 a) -> p dmc bb g8 a h",
                        dmc=8, h=4, bb=8, g8=8)[:, oc, B8]
                    e.tensor_scalar(
                        out=dst,
                        in0=pb[2 + oc % 2][:, 0:256].rearrange(
                            "p (g8 a h) -> p g8 a h", g8=8, a=8),
                        scalar1=s.boc[:, oc:oc + 1], scalar2=0.0,
                        op0=A.add, op1=A.add)
                    e.drain()
                    S("woev").inc(e.nop())

        @block.sync
        def _(e):
            for half8 in range(2):
                B8 = q * 2 + half8
                for oc in range(8):
                    wi = B8 * 8 + oc
                    if half8 == 0 and oc == 0:
                        e.wait_ge(S("vdmp").h, 16 * 32 * (q + 1))
                    if wi >= 1:
                        e.wait_ge(S("womm").h, 2 * wi)
                    _w(e, S("wold")) or S("wold").incd(e.dma_start(
                        out=s.sc(8192, 2048).rearrange("p (kk m) -> p kk m", kk=16),
                        in_=d["Wo"][0:2048, oc * 128:(oc + 1) * 128]
                        .rearrange("(kk p) m -> p kk m", p=128)))
                    e.wait_ge(S("womm").h, 2 * wi + 1)
                    _w(e, S("wold")) or S("wold").incd(e.dma_start(
                        out=s.sc(8192, 2048).rearrange("p (kk m) -> p kk m", kk=16),
                        in_=d["Wo"][2048:4096, oc * 128:(oc + 1) * 128]
                        .rearrange("(kk p) m -> p kk m", p=128)))

    def phXPOST(self, block):
        S, s, d = self.S, self, self.d

        @block.sync
        def _(e):
            e.wait_ge(S("vmm").h, 128)
            e.wait_ge(S("qkmm").h, 256)
            S("avmm").wait(e)
            S("vreld").wait(e)
            S("xpld").incd(e.dma_start(
                out=s.buf_a[:, :].rearrange("p (c t) -> p c t", c=8),
                in_=d["x_postT"][:, :].rearrange("(c p) t -> p c t", p=128)))

        self.ln(block, s.buf_a, s.buf_a, "lnp", S("xpld"),
                pregate=("woev", "avev", "smev", "vevst"))
        self.arx(block, "ar1", s.rm16b[:, 0:1], 1, [s.col("t0")], S("lnpvar"))

        @block.vector
        def _(e):
            S("ar1col").wait(e)
            e.tensor_scalar(out=s.col("sx"), in0=s.col("t0"), scalar1=3.0518509447574615e-05,
                            scalar2=1e-12, op0=A.mult, op1=A.max)
            e.drain()
            e.reciprocal(s.col("rx"), s.col("sx"))
            e.drain()
            S("rxcol").inc(e.nop())
            S("lnpdone").wait(e)
            for kc in range(8):
                sl = s.buf_a[:, kc * T:(kc + 1) * T]
                e.tensor_scalar(out=sl, in0=sl, scalar1=s.col("rx"), scalar2=MAGIC,
                                op0=A.mult, op1=A.add)
                e.drain()
                e.tensor_scalar(out=sl, in0=sl, scalar1=MAGIC, scalar2=0.0,
                                op0=A.subtract, op1=A.add)
                e.drain()
                S("xnq").inc(e.nop())

    def phUPGATE(self, block):
        S, s, d, pb = self.S, self, self.d, self.pb

        @block.sync
        def _(e):
            e.wait_ge(S("cc").h, 7)
            e.wait_ge(S("woev").h, 64)
            for it in range(48):
                w, mt = it // 24, it % 24
                ag = d["agup_o"] if w == 0 else d["aggt_o"]
                if it >= 2:
                    e.wait_ge(S("ugmm").h, 4 * (it - 1))
                _w(e, S("wqlld")) or S("wqlld").incd(e.dma_start(
                    out=s.sc((it % 2) * 1024, 1024)
                    .rearrange("p (c m) -> p c m", c=8),
                    in_=ag[mt // 3, :, (mt % 3) * 128:(mt % 3 + 1) * 128]
                    .rearrange("(c p) m -> p c m", p=128)))

        @block.tensor
        def _(e):
            S("xnq").wait(e)
            S("lnpvar").wait(e)
            for it in range(48):
                e.wait_ge(S("wqlld").h, 16 * (it + 1))
                for ns in range(4):
                    if it >= 1:
                        e.wait_ge(S("ugev").h, (it - 1) * 4 + ns + 1)
                    for kc in range(8):
                        i = e.matmul(pb[ns][:, :],
                                     s.sc((it % 2) * 1024 + kc * 128, 128),
                                     s.buf_a[:, kc * T + ns * 512: kc * T + (ns + 1) * 512],
                                     start=(kc == 0), stop=(kc == 7))
                    S("ugmm").inc(i)

        @block.vector
        def _(e):
            for it in range(48):
                w = it // 24
                if it >= 1:
                    S("ugdmp").wait(e, 16 * it)
                for ns in range(4):
                    S("ugmm").wait(e, 4 * it + ns + 1)
                    e.tensor_copy(s.sc(2048 + ns * 512, 512), pb[ns][:, :])
                    e.drain()
                    e.tensor_reduce(out=s.rm16b[:, 11:12], in_=pb[ns][:, :],
                                    axis=AX, op=A.max, apply_absolute_value=True)
                    e.drain()
                    if it % 24 == 0 and ns == 0:
                        e.tensor_copy(s.rm16[:, w:w + 1], s.rm16b[:, 11:12])
                    else:
                        e.tensor_tensor(out=s.rm16[:, w:w + 1], in0=s.rm16[:, w:w + 1],
                                        in1=s.rm16b[:, 11:12], op=A.max)
                    e.drain()
                    S("ugev").inc(e.nop())

        @block.gpsimd
        def _(e):
            for it in range(48):
                w, mt = it // 24, it % 24
                dst = d["updump"] if w == 0 else d["gatedump"]
                S("ugev").wait(e, 4 * (it + 1))
                _w(e, S("ugdmp")) or S("ugdmp").incd(e.dma_start(
                    out=dst[mt * 128:(mt + 1) * 128, :], in_=s.sc(2048, 2048)))

        self.arx(block, "ar2", s.rm16[:, 0:2], 2, [s.col("t0"), s.col("t1")],
                 S("ugev"))

        @block.vector
        def _(e):
            S("ar2col").wait(e)
            e.tensor_scalar(out=s.col("sup"), in0=s.col("t0"), scalar1=s.col("sx"),
                            scalar2=3.0518509447574615e-05, op0=A.mult, op1=A.mult)
            e.drain()
            e.tensor_scalar(out=s.col("sup"), in0=s.col("sup"), scalar1=1e-12,
                            scalar2=0.0, op0=A.max, op1=A.add)
            e.drain()
            e.reciprocal(s.col("t0"), s.col("sup"))
            e.drain()
            e.tensor_tensor(out=s.col("cup"), in0=s.col("sx"), in1=s.col("t0"),
                            op=A.mult)
            e.drain()
            e.tensor_scalar(out=s.col("sgt"), in0=s.col("t1"), scalar1=s.col("sx"),
                            scalar2=3.0518509447574615e-05, op0=A.mult, op1=A.mult)
            e.drain()
            e.tensor_scalar(out=s.col("sgt"), in0=s.col("sgt"), scalar1=1e-12,
                            scalar2=0.0, op0=A.max, op1=A.add)
            e.drain()
            e.reciprocal(s.col("t1"), s.col("sgt"))
            e.drain()
            e.tensor_tensor(out=s.col("cgt"), in0=s.col("sx"), in1=s.col("t1"),
                            op=A.mult)
            e.drain()
            e.tensor_scalar(out=s.col("c2"), in0=s.col("sgt"), scalar1=1.52587890625e-05,
                            scalar2=0.0, op0=A.mult, op1=A.add)
            e.drain()
            S("ar2cols").inc(e.nop())

    def phF(self, block):
        S, s, d, pb = self.S, self, self.d, self.pb
        ba = s.buf_a

        def up_sl(ch):
            return ba[:, (ch % 2) * 2048:(ch % 2) * 2048 + 2048]

        def gt_sl(ch):
            return ba[:, 4096 + (ch % 2) * 2048:4096 + (ch % 2) * 2048 + 2048]

        def t_sl(ch):
            return ba[:, 8192 + (ch % 2) * 2048:8192 + (ch % 2) * 2048 + 2048]

        @block.gpsimd
        def _(e):
            e.wait_ge(S("ugdmp").h, 768)
            for ch in range(24):
                if ch >= 2:
                    e.wait_ge(S("f2dmp").h, 32 * (ch - 1))
                _w(e, S("f2ld")) or S("f2ld").incd(e.dma_start(out=up_sl(ch),
                                           in_=d["updump"][ch * 128:(ch + 1) * 128, :]))
                _w(e, S("f2ld")) or S("f2ld").incd(e.dma_start(out=gt_sl(ch),
                                           in_=d["gatedump"][ch * 128:(ch + 1) * 128, :]))
                if ch >= 1:
                    S("f2v").wait(e, ch)
                    _w(e, S("f2dmp")) or S("f2dmp").incd(e.dma_start(
                        out=d["nupdump"][(ch - 1) * 128:ch * 128, :], in_=up_sl(ch - 1)))
                    _w(e, S("f2dmp")) or S("f2dmp").incd(e.dma_start(
                        out=d["pdump"][(ch - 1) * 128:ch * 128, :], in_=t_sl(ch - 1)))
            S("f2v").wait(e, 24)
            _w(e, S("f2dmp")) or S("f2dmp").incd(e.dma_start(out=d["nupdump"][23 * 128:24 * 128, :],
                                        in_=up_sl(23)))
            _w(e, S("f2dmp")) or S("f2dmp").incd(e.dma_start(out=d["pdump"][23 * 128:24 * 128, :],
                                        in_=t_sl(23)))

        @block.vector
        def _(e):
            S("ar2cols").wait(e)
            for ch in range(24):
                S("f2ld").wait(e, 32 * (ch + 1))
                e.tensor_scalar(out=up_sl(ch), in0=up_sl(ch), scalar1=s.col("cup"),
                                scalar2=MAGIC, op0=A.mult, op1=A.add)
                e.drain()
                e.tensor_scalar(out=up_sl(ch), in0=up_sl(ch), scalar1=MAGIC,
                                scalar2=0.0, op0=A.subtract, op1=A.add)
                e.drain()
                e.tensor_scalar(out=gt_sl(ch), in0=gt_sl(ch), scalar1=s.col("cgt"),
                                scalar2=MAGIC, op0=A.mult, op1=A.add)
                e.drain()
                e.tensor_scalar(out=gt_sl(ch), in0=gt_sl(ch), scalar1=MAGIC,
                                scalar2=0.0, op0=A.subtract, op1=A.add)
                e.drain()
                S("f2g").inc(e.nop())
                S("f2s").wait(e, ch + 1)
                e.tensor_scalar(out=t_sl(ch), in0=t_sl(ch), scalar1=65536.0,
                                scalar2=MAGIC, op0=A.mult, op1=A.add)
                e.drain()
                e.tensor_scalar(out=t_sl(ch), in0=t_sl(ch), scalar1=MAGIC,
                                scalar2=65535.0, op0=A.subtract, op1=A.min)
                e.drain()
                e.tensor_tensor(out=t_sl(ch), in0=gt_sl(ch), in1=t_sl(ch), op=A.mult)
                e.drain()
                e.tensor_reduce(out=s.rm16b[:, 11:12], in_=t_sl(ch), axis=AX,
                                op=A.max, apply_absolute_value=True)
                e.drain()
                if ch == 0:
                    e.tensor_copy(s.rm16[:, 2:3], s.rm16b[:, 11:12])
                else:
                    e.tensor_tensor(out=s.rm16[:, 2:3], in0=s.rm16[:, 2:3],
                                    in1=s.rm16b[:, 11:12], op=A.max)
                e.drain()
                S("f2v").inc(e.nop())

        @block.scalar
        def _(e):
            for ch in range(24):
                S("f2g").wait(e, ch + 1)
                i = e.activation(out=t_sl(ch), in_=gt_sl(ch), func=AF.Sigmoid,
                                 scale=s.col("sgt"))
                S("f2s").inc(i)
                e.drain()

        self.arx(block, "ar3", s.rm16[:, 2:3], 1, [s.col("t0")], S("f2v"))

        @block.vector
        def _(e):
            S("ar3col").wait(e)
            e.tensor_scalar(out=s.col("t0"), in0=s.col("t0"), scalar1=s.col("c2"),
                            scalar2=3.0518509447574615e-05, op0=A.mult, op1=A.mult)
            e.drain()
            e.tensor_scalar(out=s.col("t0"), in0=s.col("t0"), scalar1=1e-12,
                            scalar2=0.0, op0=A.max, op1=A.add)
            e.drain()
            e.tensor_tensor(out=s.col("c3"), in0=s.col("t0"), in1=s.col("sup"),
                            op=A.mult)
            e.drain()
            e.reciprocal(s.col("t1"), s.col("t0"))
            e.drain()
            e.tensor_tensor(out=s.col("c2r2"), in0=s.col("c2"), in1=s.col("t1"),
                            op=A.mult)
            e.drain()
            S("ar3cols").inc(e.nop())

        @block.gpsimd
        def _(e):
            e.wait_ge(S("f2dmp").h, 768)
            e.wait_ge(S("f2v").h, 24)
            for ch in range(24):
                if ch >= 2:
                    e.wait_ge(S("f3dmp").h, 1152 - 384 + 16 * (ch - 1))
                _w(e, S("f3ld")) or S("f3ld").incd(e.dma_start(out=up_sl(ch),
                                           in_=d["pdump"][ch * 128:(ch + 1) * 128, :]))
                _w(e, S("f3ld")) or S("f3ld").incd(e.dma_start(out=gt_sl(ch),
                                           in_=d["nupdump"][ch * 128:(ch + 1) * 128, :]))
                if ch >= 1:
                    S("f3v").wait(e, 24 + ch)
                    _w(e, S("f3dmp")) or S("f3dmp").incd(e.dma_start(
                        out=d["qdump"][(ch - 1) * 128:ch * 128, :], in_=t_sl(ch - 1)))
            S("f3v").wait(e, 48)
            _w(e, S("f3dmp")) or S("f3dmp").incd(e.dma_start(out=d["qdump"][23 * 128:24 * 128, :],
                                        in_=t_sl(23)))

        @block.vector
        def _(e):
            S("ar3cols").wait(e)
            e.wait_ge(S("f2dmp").h, 768)
            for ch in range(24):
                S("f3ld").wait(e, 768 + 32 * (ch + 1))
                e.tensor_scalar(out=up_sl(ch), in0=up_sl(ch), scalar1=s.col("c2r2"),
                                scalar2=MAGIC, op0=A.mult, op1=A.add)
                e.drain()
                e.tensor_scalar(out=up_sl(ch), in0=up_sl(ch), scalar1=MAGIC,
                                scalar2=0.0, op0=A.subtract, op1=A.add)
                e.drain()
                e.tensor_tensor(out=t_sl(ch), in0=up_sl(ch), in1=gt_sl(ch), op=A.mult)
                e.drain()
                e.tensor_reduce(out=s.rm16b[:, 11:12], in_=t_sl(ch), axis=AX,
                                op=A.max, apply_absolute_value=True)
                e.drain()
                if ch == 0:
                    e.tensor_copy(s.rm16[:, 3:4], s.rm16b[:, 11:12])
                else:
                    e.tensor_tensor(out=s.rm16[:, 3:4], in0=s.rm16[:, 3:4],
                                    in1=s.rm16b[:, 11:12], op=A.max)
                e.drain()
                S("f3v").inc(e.nop())

        self.arx(block, "ar4", s.rm16[:, 3:4], 1, [s.col("t0")], S("f3v"))

        @block.vector
        def _(e):
            S("ar4col").wait(e)
            e.tensor_scalar(out=s.col("t0"), in0=s.col("t0"), scalar1=s.col("c3"),
                            scalar2=3.0518509447574615e-05, op0=A.mult, op1=A.mult)
            e.drain()
            e.tensor_scalar(out=s.col("so"), in0=s.col("t0"), scalar1=1e-12,
                            scalar2=0.0, op0=A.max, op1=A.add)
            e.drain()
            e.reciprocal(s.col("t1"), s.col("so"))
            e.drain()
            e.tensor_tensor(out=s.col("c3ro"), in0=s.col("c3"), in1=s.col("t1"),
                            op=A.mult)
            e.drain()
            S("ar4cols").inc(e.nop())

        @block.gpsimd
        def _(e):
            e.wait_ge(S("f3dmp").h, 1152)
            e.wait_ge(S("f3v").h, 48)
            for ch in range(24):
                if ch >= 2:
                    e.wait_ge(S("f4dmp").h, 1152 + 16 * (ch - 1))
                _w(e, S("f4ld")) or S("f4ld").incd(e.dma_start(out=up_sl(ch),
                                           in_=d["qdump"][ch * 128:(ch + 1) * 128, :]))
                if ch >= 1:
                    S("f4v").wait(e, 48 + ch)
                    _w(e, S("f4dmp")) or S("f4dmp").incd(e.dma_start(
                        out=d["nodump"][(ch - 1) * 128:ch * 128, :],
                        in_=up_sl(ch - 1)))
            S("f4v").wait(e, 72)
            _w(e, S("f4dmp")) or S("f4dmp").incd(e.dma_start(out=d["nodump"][23 * 128:24 * 128, :],
                                        in_=up_sl(23)))

        @block.vector
        def _(e):
            S("ar4cols").wait(e)
            e.wait_ge(S("f3dmp").h, 1152)
            for ch in range(24):
                S("f4ld").wait(e, 1536 + 16 * (ch + 1))
                e.tensor_scalar(out=up_sl(ch), in0=up_sl(ch), scalar1=s.col("c3ro"),
                                scalar2=MAGIC, op0=A.mult, op1=A.add)
                e.drain()
                e.tensor_scalar(out=up_sl(ch), in0=up_sl(ch), scalar1=MAGIC,
                                scalar2=0.0, op0=A.subtract, op1=A.add)
                e.drain()
                S("f4v").inc(e.nop())

    def phDOWN(self, block):
        S, s, d, pb = self.S, self, self.d, self.pb

        @block.gpsimd
        def _(e):
            e.wait_ge(S("f4dmp").h, 1152 + 16 * 24)
            for nT in range(8):
                if nT >= 1:
                    e.wait_ge(S("dnmm").h, 192 * nT)
                _w(e, S("norld")) or S("norld").incd(e.dma_start(
                    out=s.sc(0, 6144).rearrange("p (c t) -> p c t", c=24),
                    in_=d["nodump"][:, nT * 256:(nT + 1) * 256]
                    .rearrange("(c p) t -> p c t", p=128)))

        @block.sync
        def _(e):
            e.wait_ge(S("cc").h, 7)
            e.wait_ge(S("f4v").h, 72)
            for nT in range(8):
                for dmc in range(8):
                    if nT + dmc > 0:
                        e.wait_ge(S("dnmm").h, 192 * nT + 24 * dmc)
                    _w(e, S("wdld")) or S("wdld").incd(e.dma_start(
                        out=s.sc(6144, 3072).rearrange("p (c m) -> p c m", c=24),
                        in_=d["agdn_o"][dmc, :, :]
                        .rearrange("(c p) m -> p c m", p=128)))

        @block.tensor
        def _(e):
            for nT in range(8):
                e.wait_ge(S("norld").h, 16 * (nT + 1))
                for dmc in range(8):
                    idx = 8 * nT + dmc
                    e.wait_ge(S("wdld").h, 16 * (idx + 1))
                    if idx >= 2:
                        e.wait_ge(S("dnev").h, idx - 1)
                    for Ic in range(24):
                        i = e.matmul(pb[4 + dmc % 2][:, 0:256],
                                     s.sc(6144 + Ic * 128, 128),
                                     s.sc(Ic * 256, 256),
                                     start=(Ic == 0), stop=(Ic == 23))
                        S("dnmm").inc(i)

        @block.scalar
        def _(e):
            S("ar4cols").wait(e)
            for nT in range(8):
                for dmc in range(8):
                    S("dnmm").wait(e, 192 * nT + 24 * (dmc + 1))
                    i = e.activation(
                        out=s.buf_a[:, dmc * T + nT * 256: dmc * T + (nT + 1) * 256],
                        in_=pb[4 + dmc % 2][:, 0:256], func=AF.Silu, scale=s.col("so"))
                    S("dnev").inc(i)
                    e.drain()

    def phCOMBINE(self, block):
        S, s, d = self.S, self, self.d

        @block.sync
        def _(e):
            e.wait_ge(S("dnev").h, 64)
            for kc in range(8):
                if kc >= 2:
                    e.wait_ge(S("cmb").h, kc - 1)
                _w(e, S("xptld")) or S("xptld").incd(e.dma_start(
                    out=s.sc(12288 + (kc % 2) * 0, 2048) if False else
                    s.sc((kc % 2) * 2048, 2048),
                    in_=d["x_postT"][:, :].rearrange("(c p) t -> p c t", p=128)[:, kc, :]))

        @block.vector
        def _(e):
            S("woev").wait(e)
            S("dnev").wait(e, 64)
            for kc in range(8):
                S("xptld").wait(e, 16 * (kc + 1))
                bb = s.buf_b[:, kc * T:(kc + 1) * T]
                e.tensor_tensor(out=bb, in0=bb, in1=s.buf_a[:, kc * T:(kc + 1) * T],
                                op=A.mult)
                e.drain()
                e.tensor_tensor(out=bb, in0=bb, in1=s.sc((kc % 2) * 2048, 2048),
                                op=A.add)
                e.drain()
                S("cmb").inc(e.nop())

    def phQKV1(self, block):
        S, s, d, pb = self.S, self, self.d, self.pb
        self.ln(block, s.buf_b, s.buf_a, "ln1", S("cmb"), pregate=("dnev", "ar4col"))

        @block.sync
        def _(e):
            e.wait_ge(S("dnmm").h, 768)
            e.wait_ge(S("cmb").h, 8)
            S("wv1ld").incd(e.dma_start(
                out=s.sc(4096, 4096).rearrange("p (c m) -> p c m", c=8),
                in_=d["wqkvT"][:, 1024:1536].rearrange("(c p) m -> p c m", p=128)))
            for f in range(8):
                if f >= 2:
                    e.wait_ge(S("qkvmm").h, 4 * (f - 1))
                _w(e, S("wqkld")) or S("wqkld").incd(e.dma_start(
                    out=s.sc((f % 2) * 1024, 1024)
                    .rearrange("p (c m) -> p c m", c=8),
                    in_=d["wqkvT"][:, f * 128:(f + 1) * 128]
                    .rearrange("(c p) m -> p c m", p=128)))

        @block.tensor
        def _(e):
            S("ln1done").wait(e)
            for f in range(8):
                e.wait_ge(S("wqkld").h, 16 * (f + 1))
                for ns in range(4):
                    if f >= 1:
                        e.wait_ge(S("qkvev").h, 4 * (f - 1) + ns + 1)
                    for kc in range(8):
                        i = e.matmul(pb[ns][:, :],
                                     s.sc((f % 2) * 1024 + kc * 128, 128),
                                     s.buf_a[:, kc * T + ns * 512: kc * T + (ns + 1) * 512],
                                     start=(kc == 0), stop=(kc == 7))
                    S("qkvmm").inc(i)
            e.wait_ge(S("wv1ld").h, 16)
            e.wait_ge(S("qkvev").h, 32)
            for mt in range(16):
                if mt >= 2:
                    e.wait_ge(S("vtev").h, mt - 1)
                for kc in range(8):
                    i = e.matmul(pb[2 + mt % 2][:, :],
                                 s.buf_a[:, kc * T + mt * 128: kc * T + (mt + 1) * 128],
                                 s.sc(4096 + kc * 512, 512),
                                 start=(kc == 0), stop=(kc == 7))
                S("vtmm").inc(i)
            for mt in range(16):
                for kc in range(8):
                    idx = mt * 8 + kc
                    if idx >= 2:
                        e.wait_ge(S("nx1ev").h, idx - 1)
                    i = e.transpose(pb[6 + kc % 2][:, 0:128],
                                    s.buf_a[:, kc * T + mt * 128: kc * T + (mt + 1) * 128],
                                    s.ident[:, :])
                    S("nx1t").inc(i)
            e.wait_ge(S("nx1c").h, 1)
            i = e.transpose(pb[6][0:16, 0:128], s.rm16b[:, 0:16], s.ident[:, :])
            S("nx1tp").inc(i)

        @block.vector
        def _(e):
            for f in range(8):
                if f >= 1:
                    S("qkdmp").wait(e, 16 * f)
                for ns in range(4):
                    S("qkvmm").wait(e, 4 * f + ns + 1)
                    if f < 4:
                        e.tensor_scalar(out=s.sc(2048 + ns * 512, 512),
                                        in0=pb[ns][:, :], scalar1=ISQ512,
                                        scalar2=0.0, op0=A.mult, op1=A.add)
                    else:
                        e.tensor_copy(s.sc(2048 + ns * 512, 512), pb[ns][:, :])
                    e.drain()
                    S("qkvev").inc(e.nop())
            for mt in range(16):
                S("vtmm").wait(e, mt + 1)
                if mt >= 1:
                    S("vtdmp").wait(e, 16 * mt)
                e.tensor_copy(s.sc(8192, 512), pb[2 + mt % 2][:, :])
                e.drain()
                S("vtev").inc(e.nop())
            for mt in range(16):
                for kc in range(8):
                    idx = mt * 8 + kc
                    S("nx1t").wait(e, idx + 1)
                    e.tensor_reduce(out=s.rm16b[:, 11:12], in_=pb[6 + kc % 2][:, 0:128],
                                    axis=AX, op=A.max, apply_absolute_value=True)
                    e.drain()
                    if kc == 0:
                        e.tensor_copy(s.rm16[:, mt:mt + 1], s.rm16b[:, 11:12])
                    else:
                        e.tensor_tensor(out=s.rm16[:, mt:mt + 1],
                                        in0=s.rm16[:, mt:mt + 1],
                                        in1=s.rm16b[:, 11:12], op=A.max)
                    e.drain()
                    S("nx1ev").inc(e.nop())
            e.tensor_scalar(out=s.rm16[:, 0:16], in0=s.rm16[:, 0:16], scalar1=1e-5,
                            scalar2=0.0, op0=A.max, op1=A.add)
            e.drain()
            e.reciprocal(s.rm16b[:, 0:16], s.rm16[:, 0:16])
            e.drain()
            e.tensor_scalar(out=s.rm16b[:, 0:16], in0=s.rm16b[:, 0:16], scalar1=127.0,
                            scalar2=0.0, op0=A.mult, op1=A.add)
            e.drain()
            e.tensor_scalar(out=s.rm16[:, 0:16], in0=s.rm16[:, 0:16], scalar1=0.007874015718698502,
                            scalar2=0.0, op0=A.mult, op1=A.add)
            e.drain()
            S("nx1c").inc(e.nop())
            S("nx1tp").wait(e)
            S("qkdmp").wait(e, 128)
            e.tensor_copy(s.sc(2048, 128)[0:16, :], pb[6][0:16, 0:128])
            e.drain()
            S("nx1rs").inc(e.nop())
            e.wait_ge(S("nx1bc").h, 32)
            S("qkdmp").wait(e, 128)
            for kc in range(8):
                if kc >= 2:
                    S("nx1dmp").wait(e, 16 * (kc - 1))
                st = s.sc(2048, 2048)
                e.tensor_tensor(out=st, in0=s.buf_a[:, kc * T:(kc + 1) * T],
                                in1=s.sc(4096, 2048), op=A.mult)
                e.drain()
                e.tensor_scalar(out=st, in0=st, scalar1=MAGIC,
                                scalar2=MAGIC, op0=A.add, op1=A.subtract)
                e.drain()
                e.tensor_scalar(
                    out=s.sc((kc % 2) * 1024, 1024).bitcast(BF16),
                    in0=st, scalar1=1.0, scalar2=0.0,
                    op0=A.mult, op1=A.add)
                e.drain()
                S("nx1v").inc(e.nop())

        @block.gpsimd
        def _(e):
            for f in range(8):
                S("qkvev").wait(e, 4 * (f + 1))
                if f < 4:
                    dst = d["q1dump"][f * 128:(f + 1) * 128, :]
                else:
                    dst = d["agk_i"][(f - 4) * 128:(f - 3) * 128, :]
                _w(e, S("qkdmp")) or S("qkdmp").incd(e.dma_start(out=dst, in_=s.sc(2048, 2048)))
            for mt in range(16):
                S("vtev").wait(e, mt + 1)
                _w(e, S("vtdmp")) or S("vtdmp").incd(e.dma_start(
                    out=d["agv_i"][mt * 128:(mt + 1) * 128, :], in_=s.sc(8192, 512)))
            S("nx1rs").wait(e)
            S("nx1bci").incd(e.dma_start(out=d["rsc_dr"][:, :], in_=s.sc(2048, 128)[0:16, :]))
            S("nx1bci").wait(e)
            S("nx1bc").incd(e.dma_start(
                out=s.sc(4096, 2048),
                in_=d["rsc_dr"][:, :].rearrange("a b -> (a b)").partition_broadcast(128)))
            for kc in range(8):
                S("nx1v").wait(e, kc + 1)
                _w(e, S("nx1dmp")) or S("nx1dmp").incd(e.dma_start(
                    out=d["nx1dump"][kc * 128:(kc + 1) * 128, :],
                    in_=s.sc((kc % 2) * 1024, 1024).bitcast(BF16)))
            S("qkdmp").wait(e)
            S("cc").inc(e.collective_compute(
                "AllGather", A.bypass, replica_groups=[[0, 1], [2, 3], [4, 5], [6, 7]],
                ins=[d["agk_i"].opt()], outs=[d["agk_o"].opt()]))
            S("vtdmp").wait(e)
            S("cc").inc(e.collective_compute(
                "AllGather", A.bypass, replica_groups=[[0, 1], [2, 3], [4, 5], [6, 7]],
                ins=[d["agv_i"].opt()], outs=[d["agv_o"].opt()]))

    def phATT1(self, block):
        S, s, d, pb = self.S, self, self.d, self.pb
        nkt = [2 * (qt // 4) + 2 for qt in range(16)]
        pre_kt = [sum(nkt[:i]) for i in range(17)]
        pre_pt = [sum(4 * n for n in nkt[:i]) for i in range(17)]
        pre_av = [sum(16 * n for n in nkt[:i]) for i in range(17)]

        @block.sync
        def _(e):
            e.wait_ge(S("nx1v").h, 8)
            e.wait_ge(S("qkdmp").h, 64)
            e.wait_ge(S("nx1dmp").h, 128)
            e.wait_ge(S("nx1bci").h, 32)
            for qt in range(16):
                h = qt // 4
                if qt >= 1:
                    e.wait_ge(S("s1ev").h, 8 * qt)
                _w(e, S("q1ld")) or S("q1ld").incd(e.dma_start(
                    out=s.sc(0, 512).rearrange("p (c m) -> p c m", c=4),
                    in_=d["q1dump"][:, qt * 128:(qt + 1) * 128]
                    .rearrange("(c p) m -> p c m", p=128)))
                for j in range(2):
                    if qt >= 1:
                        e.wait_ge(S("simev").h, pre_kt[qt])
                    _w(e, S("lmld")) or S("lmld").incd(e.dma_start(
                        out=s.sc(12800 + j * 512, 512),
                        in_=d["l1mask"][:, (qt % 4) * 1024 + j * 512:
                                        (qt % 4) * 1024 + (j + 1) * 512]))
                e.wait_ge(S("cc").h, 12)
                for kt in range(nkt[qt]):
                    ki = pre_kt[qt] + kt
                    if ki >= 2:
                        e.wait_ge(S("simmm").h, ki - 1)
                    if qt >= 1:
                        e.wait_ge(S("avmm1").h, pre_av[qt])
                    if qt >= 1:
                        e.wait_ge(S("projmm").h, 8 * qt)
                    _w(e, S("kvld")) or S("kvld").incd(e.dma_start(
                        out=s.sc(512 + (kt % 2) * 2048, 2048)
                        .rearrange("p (c t) -> p c t", c=4),
                        in_=d["agk_o"][kt % 2, :, (kt // 2) * 512:(kt // 2 + 1) * 512]
                        .rearrange("(c p) t -> p c t", p=128)))
                e.wait_ge(S("avmm1").h, pre_av[qt + 1])
                for kc2 in range(4):
                    _w(e, S("woutld")) or S("woutld").incd(e.dma_start(
                        out=s.sc(512 + kc2 * 1024, 1024),
                        in_=d["woutT"][kc2 * 128:(kc2 + 1) * 128, :]))

        @block.scalar
        def _(e):
            for qt in range(16):
                h = qt // 4
                S("sm1x").wait(e, qt + 1)
                i = e.activation(out=s.sc(8704, nkt[qt] * 512),
                                 in_=s.sc(4608, nkt[qt] * 512), func=AF.Exp,
                                 bias=s.rm16b[:, 14:15], accum_out=s.rm16b[:, 13:14])
                S("sm1e").inc(i)
                e.drain()
                e.wait_ge(S("cc").h, 13)
                for kt in range(nkt[qt]):
                    ki = pre_kt[qt] + kt
                    e.wait_ge(S("simmm").h, pre_kt[qt + 1])
                    if ki >= 2:
                        e.wait_ge(S("avmm1").h, pre_av[qt] + 16 * (kt - 1))
                    _w(e, S("vld1")) or S("vld1").incd(e.dma_start(
                        out=s.sc(512 + (kt % 2) * 2048, 2048)
                        .rearrange("p (c v) -> p c v", c=4),
                        in_=d["agv_o"][kt % 2, :, :]
                        .rearrange("(s c p) v -> p s c v", c=4, p=128)[
                            :, kt // 2, :, :]))

        @block.tensor
        def _(e):
            e.wait_ge(S("qkvev").h, 32)
            e.wait_ge(S("vtev").h, 16)
            for qt in range(16):
                for kt in range(nkt[qt]):
                    ki = pre_kt[qt] + kt
                    e.wait_ge(S("q1ld").h, 16 * (qt + 1))
                    e.wait_ge(S("kvld").h, 16 * (ki + 1))
                    if ki >= 2:
                        e.wait_ge(S("simev").h, ki - 1)
                    for kc in range(4):
                        i = e.matmul(pb[kt % 2][:, :],
                                     s.sc(kc * 128, 128),
                                     s.sc(512 + (kt % 2) * 2048 + kc * 512, 512),
                                     start=(kc == 0), stop=(kc == 3))
                    S("simmm").inc(i)
                for c in range(nkt[qt] * 4):
                    pc = pre_pt[qt] + c
                    S("pn1").wait(e, qt + 1)
                    if pc >= 2:
                        e.wait_ge(S("ptev").h, pc - 1)
                    i = e.transpose(pb[2 + pc % 2][:, 0:128],
                                    s.sc(8704 + c * 128, 128), s.ident[:, :])
                    S("ptmm").inc(i)
                e.wait_ge(S("ptev").h, pre_pt[qt + 1])
                if qt >= 1:
                    e.wait_ge(S("avev1").h, qt)
                for kt in range(nkt[qt]):
                    e.wait_ge(S("vld1").h, 16 * (pre_kt[qt] + kt + 1))
                    for cc4 in range(4):
                        for dvM in range(4):
                            i = e.matmul(
                                pb[4 + dvM][:, 0:128],
                                s.sc(512 + (kt % 2) * 2048 + cc4 * 512 + dvM * 128, 128),
                                s.sc(8704 + (kt * 4 + cc4) * 128, 128),
                                start=(kt == 0 and cc4 == 0),
                                stop=(kt == nkt[qt] - 1 and cc4 == 3))
                            S("avmm1").inc(i)
                e.wait_ge(S("avev1").h, qt + 1)
                e.wait_ge(S("woutld").h, 64 * (qt + 1))
                if qt == 0:
                    e.wait_ge(S("nx1ev").h, 128)
                    e.wait_ge(S("nx1rs").h, 1)
                for oc in range(8):
                    if 8 * qt + oc >= 2:
                        e.wait_ge(S("s1ev").h, 8 * qt + oc - 1)
                    for kc2 in range(4):
                        i = e.matmul(pb[6 + oc % 2][:, 0:128],
                                     s.sc(512 + kc2 * 1024 + oc * 128, 128),
                                     s.sc(4608 + kc2 * 128, 128),
                                     start=(kc2 == 0), stop=(kc2 == 3))
                    S("projmm").inc(i)

        @block.vector
        def _(e):
            for qt in range(16):
                h = qt // 4
                for kt in range(nkt[qt]):
                    ki = pre_kt[qt] + kt
                    S("simmm").wait(e, ki + 1)
                    S("lmld").wait(e, 32 * (qt + 1))
                    dst = s.sc(4608 + kt * 512, 512)
                    if kt < 2 * h:
                        e.tensor_copy(dst, pb[kt % 2][:, :])
                    else:
                        e.scalar_tensor_tensor(out=dst, in0=pb[kt % 2][:, :],
                                               scalar=0.0,
                                               in1=s.sc(12800 + (kt - 2 * h) * 512, 512),
                                               op0=A.add, op1=A.add)
                    e.drain()
                    S("simev").inc(e.nop())
                if qt >= 1:
                    S("sm1e").wait(e, qt)
                e.tensor_reduce(out=s.rm16b[:, 14:15], in_=s.sc(4608, nkt[qt] * 512),
                                axis=AX, op=A.max, negate=True)
                e.drain()
                S("sm1x").inc(e.nop())
                S("sm1e").wait(e, qt + 1)
                e.reciprocal(s.rm16b[:, 12:13], s.rm16b[:, 13:14])
                e.drain()
                e.tensor_scalar(out=s.sc(8704, nkt[qt] * 512),
                                in0=s.sc(8704, nkt[qt] * 512),
                                scalar1=s.rm16b[:, 12:13], scalar2=0.0,
                                op0=A.mult, op1=A.add)
                e.drain()
                S("pn1").inc(e.nop())
                for c in range(nkt[qt] * 4):
                    pc = pre_pt[qt] + c
                    S("ptmm").wait(e, pc + 1)
                    e.tensor_copy(s.sc(8704 + c * 128, 128), pb[2 + pc % 2][:, 0:128])
                    e.drain()
                    S("ptev").inc(e.nop())
                S("avmm1").wait(e, pre_av[qt + 1])
                for dvM in range(4):
                    e.tensor_copy(s.sc(4608 + dvM * 128, 128), pb[4 + dvM][:, 0:128])
                    e.drain()
                S("avev1").inc(e.nop())
                for oc in range(8):
                    S("projmm").wait(e, 8 * qt + oc + 1)
                    e.tensor_scalar(out=s.buf_a[:, oc * T + qt * 128:
                                                oc * T + (qt + 1) * 128],
                                    in0=pb[6 + oc % 2][:, 0:128],
                                    scalar1=s.bouc[:, oc:oc + 1], scalar2=0.0,
                                    op0=A.add, op1=A.add)
                    e.drain()
                    S("s1ev").inc(e.nop())

    def phFFN1(self, block):
        S, s, d, pb = self.S, self, self.d, self.pb
        nx1b = lambda: s.sc(0, 1024).bitcast(BF16)
        tern = lambda sl: s.sc(1024 + sl * 256, 256).bitcast(BF16)
        npT = lambda: s.sc(9728, 4096).bitcast(BF16)
        gsil = lambda: s.nmb[:, :]
        upv = lambda: s.rsb[:, :]
        dqb = lambda: s.rowbuf[:, 2176:2432]
        dqst = lambda: s.rowbuf[:, 2048:2176]
        wd1 = lambda: s.rowbuf[:, 0:2048].bitcast(BF16)

        def pt(ms):
            return s.sc(1536 + ms * 4096, 4096)

        @block.sync
        def _(e):
            e.wait_ge(S("nx1dmp").h, 128)
            e.wait_ge(S("s1ev").h, 128)
            for m in range(8):
                if m >= 1:
                    e.wait_ge(S("f1mm").h, 256 * m)
                _w(e, S("nxld")) or S("nxld").incd(e.dma_start(
                    out=s.sc(0, 1024).bitcast(BF16).rearrange("p (c t) -> p c t", c=8),
                    in_=d["nx1dump"][:, m * 256:(m + 1) * 256]
                    .rearrange("(c p) t -> p c t", p=128)))
                for w in range(2):
                    ag = d["agg1_o"] if w == 0 else d["agu1_o"]
                    for nI in range(8):
                        for kc in range(8):
                            gidx = m * 128 + w * 64 + nI * 8 + kc
                            if gidx >= 2:
                                e.wait_ge(S("f1mm").h, 2 * (gidx - 1))
                            _w(e, S("ternld")) or S("ternld").incd(e.dma_start(
                                out=tern(gidx % 2),
                                in_=ag[kc, :, nI * 512:(nI + 1) * 512]))
                for dmc in range(8):
                    di = m * 8 + dmc
                    if di >= 1:
                        e.wait_ge(S("dn1mm").h, 32 * di)
                    _w(e, S("wdld1")) or S("wdld1").incd(e.dma_start(
                        out=wd1().rearrange("p (c m) -> p c m", c=32),
                        in_=d["agd1_o"][:, :, dmc * 128:(dmc + 1) * 128]
                        .rearrange("r (c p) m -> p (r c) m", p=128)))
                S("dn1ev").wait(e, 8 * (m + 1))
                S("stout").incd(e.dma_start(
                    out=d["out_T"][:, :].rearrange("(c p) t -> p c t", p=128)[
                        :, :, m * 256:(m + 1) * 256],
                    in_=s.buf_b[:, :].rearrange("p (c t) -> p c t", c=8)[
                        :, :, m * 256:(m + 1) * 256]))

        @block.tensor
        def _(e):
            e.wait_ge(S("cc").h, 13)
            e.wait_ge(S("s1ev").h, 128)
            for m in range(8):
                e.wait_ge(S("nxld").h, 16 * (m + 1))
                for w in range(2):
                    for nI in range(8):
                        pidx = m * 16 + w * 8 + nI
                        if pidx >= 1:
                            e.wait_ge(S("f1free").h, 2 * pidx)
                        for kc in range(8):
                            gidx = m * 128 + w * 64 + nI * 8 + kc
                            e.wait_ge(S("ternld").h, 16 * (gidx + 1))
                            for ms in range(2):
                                i = e.matmul(pb[ms][:, :],
                                             nx1b()[:, kc * 256 + ms * 128:
                                                    kc * 256 + (ms + 1) * 128],
                                             tern(gidx % 2),
                                             start=(kc == 0), stop=(kc == 7))
                                S("f1mm").inc(i)
                for ms in range(2):
                    S("np1").wait(e, 2 * m + ms + 1)
                    for Ic in range(32):
                        tidx = (m * 2 + ms) * 32 + Ic
                        if tidx >= 2:
                            e.wait_ge(S("nptev").h, tidx - 1)
                        i = e.transpose(pb[2 + Ic % 2][:, 0:128],
                                        pt(ms)[:, Ic * 128:(Ic + 1) * 128],
                                        s.ident[:, :])
                        S("nptmm").inc(i)
                S("dqc").wait(e, m + 1)
                i = e.transpose(pb[6][0:2, 0:128], s.rm16b[:, 8:10], s.ident[:, :])
                S("dqtp").inc(i)
                e.wait_ge(S("nptev").h, 64 * (m + 1))
                for dmc in range(8):
                    di = m * 8 + dmc
                    e.wait_ge(S("wdld1").h, 16 * (di + 1))
                    if di >= 2:
                        e.wait_ge(S("dn1ev").h, di - 1)
                    for Ic in range(32):
                        i = e.matmul(pb[4 + dmc % 2][:, 0:256],
                                     wd1()[:, Ic * 128:(Ic + 1) * 128],
                                     npT()[:, Ic * 256:(Ic + 1) * 256],
                                     start=(Ic == 0), stop=(Ic == 31))
                        S("dn1mm").inc(i)

        @block.scalar
        def _(e):
            for m in range(8):
                S("f1cols").wait(e, m + 1)
                if m >= 1:
                    e.wait_ge(S("nptmm").h, 64 * m)
                for nI in range(8):
                    for ms in range(2):
                        e.wait_ge(S("f1mm").h, 256 * m + 16 * nI + 16)
                        i = e.activation(out=pt(ms)[:, nI * 512:(nI + 1) * 512],
                                         in_=pb[ms][:, :], func=AF.Silu,
                                         scale=s.rm16b[:, 4 + ms:5 + ms])
                        S("f1free").inc(i)
                        e.drain()
                for dmc in range(8):
                    di = m * 8 + dmc
                    S("f1dn").wait(e, di + 1)
                    i = e.activation(out=upv()[:, 0:256], in_=gsil()[:, 0:256],
                                     func=AF.Silu)
                    S("f1dn2").inc(i)
                    e.drain()

        @block.vector
        def _(e):
            for m in range(8):
                for ms in range(2):
                    e.tensor_tensor(out=s.rm16b[:, 4 + ms:5 + ms],
                                    in0=s.rm16[:, 2 * m + ms:2 * m + ms + 1],
                                    in1=s.col("mg1"), op=A.mult)
                    e.drain()
                    e.tensor_tensor(out=s.rm16b[:, 6 + ms:7 + ms],
                                    in0=s.rm16[:, 2 * m + ms:2 * m + ms + 1],
                                    in1=s.col("mu1"), op=A.mult)
                    e.drain()
                S("f1cols").inc(e.nop())
                for nI in range(8):
                    for ms in range(2):
                        e.wait_ge(S("f1mm").h, 256 * m + 128 + 16 * nI + 16)
                        e.wait_ge(S("f1free").h, 16 * m * 2 + 2 * nI + ms + 1)
                        e.tensor_scalar(out=upv(), in0=pb[ms][:, :],
                                        scalar1=s.rm16b[:, 6 + ms:7 + ms],
                                        scalar2=0.0, op0=A.mult, op1=A.add)
                        e.drain()
                        psl = pt(ms)[:, nI * 512:(nI + 1) * 512]
                        e.tensor_tensor(out=psl, in0=psl, in1=upv(), op=A.mult)
                        e.drain()
                        S("f1free").inc(e.nop())
                for ms in range(2):
                    e.tensor_reduce(out=s.rm16b[:, 11:12], in_=pt(ms), axis=AX,
                                    op=A.max, apply_absolute_value=True)
                    e.drain()
                    e.tensor_scalar(out=s.rm16b[:, 11:12], in0=s.rm16b[:, 11:12],
                                    scalar1=1e-5, scalar2=0.0, op0=A.max, op1=A.add)
                    e.drain()
                    e.reciprocal(s.rm16b[:, 10:11], s.rm16b[:, 11:12])
                    e.drain()
                    e.tensor_scalar(out=s.rm16b[:, 10:11], in0=s.rm16b[:, 10:11],
                                    scalar1=127.0, scalar2=0.0, op0=A.mult, op1=A.add)
                    e.drain()
                    e.tensor_scalar(out=s.rm16b[:, 8 + ms:9 + ms],
                                    in0=s.rm16b[:, 11:12],
                                    scalar1=0.007874015718698502, scalar2=0.0, op0=A.mult, op1=A.add)
                    e.drain()
                    e.tensor_scalar(out=pt(ms), in0=pt(ms),
                                    scalar1=s.rm16b[:, 10:11], scalar2=MAGIC,
                                    op0=A.mult, op1=A.add)
                    e.drain()
                    S("np1").inc(e.nop())
                S("dqc").inc(e.nop())
                for ms in range(2):
                    for Ic in range(32):
                        tidx = (m * 2 + ms) * 32 + Ic
                        S("nptmm").wait(e, tidx + 1)
                        e.tensor_scalar(out=npT()[:, Ic * 256 + ms * 128:
                                                  Ic * 256 + ms * 128 + 128],
                                        in0=pb[2 + Ic % 2][:, 0:128],
                                        scalar1=MAGIC, scalar2=0.0,
                                        op0=A.subtract, op1=A.add)
                        e.drain()
                        S("nptev").inc(e.nop())
                S("dqtp").wait(e, m + 1)
                if m >= 1:
                    S("dqbci").wait(e, 16 * m)
                e.tensor_copy(dqst()[0:2, 0:128], pb[6][0:2, 0:128])
                e.drain()
                S("dqrs").inc(e.nop())
                S("dqbc").wait(e, 16 * (m + 1))
                e.tensor_scalar(out=dqb(), in0=dqb(), scalar1=s.col("md1"),
                                scalar2=0.0, op0=A.mult, op1=A.add)
                e.drain()
                for dmc in range(8):
                    di = m * 8 + dmc
                    S("dn1mm").wait(e, 32 * (di + 1))
                    e.scalar_tensor_tensor(out=gsil()[:, 0:256],
                                           in0=pb[4 + dmc % 2][:, 0:256], scalar=0.0,
                                           in1=dqb(), op0=A.add, op1=A.mult)
                    e.drain()
                    S("f1dn").inc(e.nop())
                    S("f1dn2").wait(e, di + 1)
                    e.scalar_tensor_tensor(out=gsil()[:, 0:256], in0=upv()[:, 0:256],
                                           scalar=0.0,
                                           in1=s.buf_a[:, dmc * T + m * 256:
                                                       dmc * T + (m + 1) * 256],
                                           op0=A.add, op1=A.mult)
                    e.drain()
                    bb = s.buf_b[:, dmc * T + m * 256: dmc * T + (m + 1) * 256]
                    e.tensor_tensor(out=bb, in0=bb, in1=gsil()[:, 0:256], op=A.add)
                    e.drain()
                    S("dn1ev").inc(e.nop())

        @block.gpsimd
        def _(e):
            for m in range(8):
                if m >= 1:
                    S("dn1ev").wait(e, 8 * m)
                S("dqrs").wait(e, m + 1)
                _w(e, S("dqbci")) or S("dqbci").incd(e.dma_start(out=d["dqz_dr"][0:2, :],
                                            in_=dqst()[0:2, 0:128]))
                S("dqbci").wait(e)
                _w(e, S("dqbc")) or S("dqbc").incd(e.dma_start(
                    out=dqb(),
                    in_=d["dqz_dr"][0:2, :].rearrange("a b -> (a b)")
                    .partition_broadcast(128)))
            S("stout").wait(e)

    def emit(self):
        with self.nc.Block() as block:
            self.ph0(block)
            self.phW(block)
            self.phLN0(block)
            for q in range(4):
                self.phQuarter(block, q)
            self.phXPOST(block)
            self.phUPGATE(block)
            self.phF(block)
            self.phDOWN(block)
            self.phCOMBINE(block)
            self.phQKV1(block)
            self.phATT1(block)
            self.phFFN1(block)
        return self


def _build():
    if "nc" not in _CACHE:
        b = B()
        b.emit()
        _CACHE["nc"] = b
    return _CACHE["nc"]


def kernel(L0_attn_x, L0_attn_phase, L0_attn_Wq, L0_attn_bq, L0_attn_Wk,
           L0_attn_bk, L0_attn_Wv, L0_attn_bv, L0_attn_Wo, L0_attn_bo,
           L0_ffn_w_gate, L0_ffn_w_up, L0_ffn_w_down, L1_attn_w_qkv,
           L1_attn_w_out, L1_attn_b_out, L1_ffn_w_gate, L1_ffn_w_up,
           L1_ffn_w_down):
    f32 = lambda a: np.ascontiguousarray(np.asarray(a, dtype=np.float32))
    x = f32(L0_attn_x)
    phase = f32(L0_attn_phase)
    Wq, bq, Wk, bk = f32(L0_attn_Wq), f32(L0_attn_bq), f32(L0_attn_Wk), f32(L0_attn_bk)
    Wv, bv, Wo, bo = f32(L0_attn_Wv), f32(L0_attn_bv), f32(L0_attn_Wo), f32(L0_attn_bo)
    wg0, wu0, wd0 = f32(L0_ffn_w_gate), f32(L0_ffn_w_up), f32(L0_ffn_w_down)
    wqkv, wout, bout = f32(L1_attn_w_qkv), f32(L1_attn_w_out), f32(L1_attn_b_out)
    wg1, wu1, wd1 = f32(L1_ffn_w_gate), f32(L1_ffn_w_up), f32(L1_ffn_w_down)

    phase_T = np.ascontiguousarray(
        phase.reshape(4, 8, 128).transpose(2, 0, 1).reshape(128, 32))
    bq_c = np.ascontiguousarray(bq.reshape(4, 8, 128).transpose(2, 0, 1).reshape(128, 32))
    bk_c = np.ascontiguousarray(bk.reshape(4, 8, 128).transpose(2, 0, 1).reshape(128, 32))
    bv_r = np.ascontiguousarray(bv.reshape(1, 4096))
    bo_c = np.ascontiguousarray(bo.reshape(8, 128).T)
    bout_c = np.ascontiguousarray(bout.reshape(8, 128).T)
    identf = np.eye(128, dtype=np.float32)
    bdmask = np.full((128, 128), NEG, dtype=np.float32)
    for t in range(32):
        bdmask[4 * t:4 * t + 4, 4 * t:4 * t + 4] = 0.0
    onescol = np.ones((128, 1), dtype=np.float32)
    onesrow = np.ones((1, 128), dtype=np.float32)
    wqkvT = np.ascontiguousarray(wqkv.T)
    woutT = np.ascontiguousarray(wout.T)
    wg1T, wu1T, wd1T = (np.ascontiguousarray(w.T) for w in (wg1, wu1, wd1))

    tri = np.zeros((4, 128, 512), dtype=np.float32)
    for i in range(4):
        qi = 128 * i + np.arange(128)[:, None]
        kj = np.arange(512)[None, :]
        tri[i] = np.where(qi >= kj, 0.0, NEG)

    in_maps = []
    for c in range(8):
        bb, half = c // 2, c % 2
        pre = np.arange(2048 * half, 2048 * half + 2048)
        hh = np.arange(4).repeat(512)
        uu = np.tile(np.arange(512), 4)
        post = hh * 1024 + 512 * half + uu
        l1m = np.zeros((128, 4096), dtype=np.float32)
        for i in range(4):
            if half == 0:
                l1m[:, i * 1024:i * 1024 + 512] = tri[i]
                l1m[:, i * 1024 + 512:(i + 1) * 1024] = NEG
            else:
                l1m[:, i * 1024:i * 1024 + 512] = 0.0
                l1m[:, i * 1024 + 512:(i + 1) * 1024] = tri[i]
        im = {
            "x_preT": np.ascontiguousarray(x[bb, pre].T),
            "x_postT": np.ascontiguousarray(x[bb, post].T),
            "Wq": Wq, "Wk": Wk, "Wv": Wv, "Wo": Wo,
            "phase_T": phase_T, "bq_c": bq_c, "bk_c": bk_c, "bv_r": bv_r,
            "bo_c": bo_c, "bout_c": bout_c,
            "wup_sh": np.ascontiguousarray(wu0[c * 384:(c + 1) * 384]),
            "wgt_sh": np.ascontiguousarray(wg0[c * 384:(c + 1) * 384]),
            "wdn_sh": np.ascontiguousarray(wd0[c * 128:(c + 1) * 128]),
            "wqkvT": wqkvT, "woutT": woutT,
            "wg1T_sh": np.ascontiguousarray(wg1T[c * 128:(c + 1) * 128]),
            "wu1T_sh": np.ascontiguousarray(wu1T[c * 128:(c + 1) * 128]),
            "wd1T_sh": np.ascontiguousarray(wd1T[c * 512:(c + 1) * 512]),
            "identf": identf,
            "bdmask": bdmask, "l1mask": l1m,
            "onescol": onescol, "onesrow": onesrow,
        }
        in_maps.append(im)

    b = _build()
    res = run_bass_kernel_spmd(b.nc, in_maps, core_ids=list(range(8)))
    out = np.zeros((4, 4096, 1024), dtype=np.float32)
    for c in range(8):
        bb, half = c // 2, c % 2
        hh = np.arange(4).repeat(512)
        uu = np.tile(np.arange(512), 4)
        post = hh * 1024 + 512 * half + uu
        out[bb, post] = res.results[c]["out_T"].T
    return out
